# version 01
# baseline (speedup 1.0000x reference)
import numpy as np
from contextlib import ExitStack
import concourse.bass as bass
import concourse.mybir as mybir
from concourse.bass_utils import run_bass_kernel_spmd

F32 = mybir.dt.float32
BF16 = mybir.dt.bfloat16
AF = mybir.ActivationFunctionType
ALU = mybir.AluOpType
AX = mybir.AxisListType

N_CORES = 8
D = 1024
TT = 4096
PB = D // 128

ENGINES = ["pe", "act", "dve", "pool", "sp"]
N_DMA_SEMS = 12
DMA_STREAMS = ["q%d" % i for i in range(N_DMA_SEMS)]
STREAMS = ENGINES + DMA_STREAMS


class Prog:
    def __init__(self, name="prog"):
        self.nc = bass.Bass("TRN2", target_bir_lowering=False)
        self.name = name
        self.ops = {e: [] for e in ENGINES}
        self.cnt = {s: 0 for s in STREAMS}
        self.seen = {e: {s: 0 for s in STREAMS} for e in ENGINES}
        self.res = {}
        self.es = ExitStack()
        self.sems = {}
        self._finalized = False
        self._store_q = []
        self._dma_rr = 0
        self._dma_rr_sw = 0
        self._waiters = {}

    def dram(self, name, shape, dtype=F32, kind="ExternalInput"):
        return self.nc.dram_tensor(name, list(shape), dtype, kind=kind).ap()

    def sbuf(self, name, shape, dtype=F32):
        return self.es.enter_context(self.nc.sbuf_tensor(name, list(shape), dtype))

    def psum(self, name, shape, dtype=F32):
        return self.es.enter_context(self.nc.psum_tensor(name, list(shape), dtype))

    def _need(self, eng, reads, writes):
        waits = {}
        def add(dep):
            if dep is None:
                return
            s, v = dep
            if s.startswith("q"):
                v = max(v, self.cnt[s])
            if self.seen[eng].get(s, 0) < v:
                if waits.get(s, 0) < v:
                    waits[s] = v
        for k in reads:
            r = self.res.get(k)
            if r:
                add(r["w"])
        for k in writes:
            r = self.res.get(k)
            if r:
                add(r["w"])
                for s, v in r["r"].items():
                    add((s, v))
        return waits

    def _commit(self, stream, eng, reads, writes, n_inc):
        self.cnt[stream] += n_inc
        v = self.cnt[stream]
        for k in reads:
            r = self.res.setdefault(k, {"w": None, "r": {}})
            if r["r"].get(stream, 0) < v:
                r["r"][stream] = v
        for k in writes:
            self.res[k] = {"w": (stream, v), "r": {}}
        return v

    def op(self, eng, emit, reads=(), writes=(), dma=False):
        if dma:
            if eng == "pool":
                k = 8 + self._dma_rr_sw % (N_DMA_SEMS - 8)
                self._dma_rr_sw += 1
            else:
                k = self._dma_rr % 8
                self._dma_rr += 1
            stream = DMA_STREAMS[k]
            n_inc = 16
        else:
            stream = eng
            n_inc = 1
        waits = self._need(eng, reads, writes)
        if dma:
            for s2, v2 in self._waiters.pop(stream, []):
                if self.seen[eng].get(s2, 0) < v2 and waits.get(s2, 0) < v2:
                    waits[s2] = v2
        for s, v in waits.items():
            if self.seen[eng].get(s, 0) < v:
                self.seen[eng][s] = v
        val = self._commit(stream, eng, reads, writes, n_inc)
        my_id = (stream, self.cnt[stream])
        for s in waits:
            if s.startswith("q"):
                self._waiters.setdefault(s, []).append(my_id)
        self.ops[eng].append((dict(waits), emit, stream, n_inc))
        return (stream, val)

    def store_marker(self, sv):
        self._store_q.append(sv)

    def build(self):
        nc = self.nc
        for s in STREAMS:
            self.sems[s] = self.es.enter_context(nc.semaphore("sem_" + s))
        block = self.es.enter_context(nc.Block())
        secs = {"pe": block.tensor, "act": block.scalar, "dve": block.vector,
                "pool": block.gpsimd, "sp": block.sync}
        eng_handles = {}

        def run_section(eng):
            def body(h):
                eng_handles[eng] = h
                for waits, emit, stream, n_inc in self.ops[eng]:
                    for s, v in waits.items():
                        h.wait_ge(self.sems[s], v)
                    inst = emit(h)
                    inst.then_inc(self.sems[stream], n_inc)
                if eng == "pool" and self._store_q:
                    for q, v in self._store_q:
                        h.wait_ge(self.sems[q], v)
            return body

        for eng in ENGINES:
            secs[eng](run_section(eng))
        self.es.close()
        self._finalized = True
        return nc


def rk(tensor, tag=""):
    return (id(tensor), tag)


def col(tile, j):
    return tile[:, j:j + 1]


class Shared:

    def __init__(self, p: "Prog", tag=""):
        self.p = p
        self.ones = p.sbuf("c_ones" + tag, (128, 128), F32)
        self.scr = [p.sbuf(f"scr{i}" + tag, (128, 512), F32) for i in range(9)]
        self.ctile = p.sbuf("c_consts" + tag, (128, 16), F32)
        self._cmap = {}

    def load(self, p, ones_d):
        p.op("sp", lambda h: h.dma_start(out=self.ones[:], in_=ones_d),
             writes=[rk(self.ones)], dma=True)

    def cst(self, value):
        value = float(value)
        if value not in self._cmap:
            j = len(self._cmap)
            assert j < 16
            self._cmap[value] = j
            self.p.op("dve", lambda h, j=j, value=value: h.memset(self.ctile[:, j:j + 1], value),
                      writes=[rk(self.ctile, j)])
        return col(self.ctile, self._cmap[value])


def emit_ln(p, sh, xc, xnb, ps_a, ps_b, stats, tc, out_off=0, want_xn=None, want_rnorm=None):
    nc = p.nc
    sq = sh.scr[:8]
    for pb in range(PB):
        t = sq[pb]
        p.op("act", lambda h, t=t, pb=pb: h.activation(t[:, :tc], xc[pb][:, :tc], AF.Square),
             reads=[rk(xc[pb])], writes=[rk(t)])

    def mm_sum(h):
        for pb in range(PB):
            i = h.matmul(ps_a[:, :tc], sh.ones[:], xc[pb][:, :tc],
                         start=(pb == 0), stop=(pb == PB - 1))
        return i
    p.op("pe", mm_sum, reads=[rk(sh.ones)] + [rk(t) for t in xc], writes=[rk(ps_a)])

    def mm_sumsq(h):
        for pb in range(PB):
            i = h.matmul(ps_b[:, :tc], sh.ones[:], sq[pb][:, :tc],
                         start=(pb == 0), stop=(pb == PB - 1))
        return i
    p.op("pe", mm_sumsq, reads=[rk(sh.ones)] + [rk(t) for t in sq], writes=[rk(ps_b)])

    mrep, msq, varr, rstd = stats["mrep"], stats["msq"], stats["varr"], stats["rstd"]
    p.op("act", lambda h: h.activation(mrep[:, :tc], ps_a[:, :tc], AF.Identity, scale=1.0 / D),
         reads=[rk(ps_a)], writes=[rk(mrep)])
    p.op("act", lambda h: h.activation(msq[:, :tc], mrep[:, :tc], AF.Square),
         reads=[rk(mrep)], writes=[rk(msq)])
    p.op("dve", lambda h: h.scalar_tensor_tensor(varr[:, :tc], ps_b[:, :tc], 1.0 / D,
                                                 msq[:, :tc], ALU.mult, ALU.subtract),
         reads=[rk(ps_b), rk(msq)], writes=[rk(varr)])
    beps = sh.cst(1e-5)
    p.op("act", lambda h: h.activation(msq[:, :tc], varr[:, :tc], AF.Ln, bias=beps),
         reads=[rk(varr), rk(sh.ctile, sh._cmap[1e-5])], writes=[rk(msq)])
    p.op("act", lambda h: h.activation(rstd[:, :tc], msq[:, :tc], AF.Exp, scale=-0.5),
         reads=[rk(msq)], writes=[rk(rstd)])
    if want_rnorm is not None:
        rn = stats["rnorm"]
        t1 = sh.scr[8]
        p.op("dve", lambda h: h.tensor_tensor(t1[:, :tc], varr[:, :tc], rstd[:, :tc], ALU.mult),
             reads=[rk(varr), rk(rstd)], writes=[rk(t1)])
        p.op("dve", lambda h: h.tensor_tensor(t1[:, :tc], t1[:, :tc], rstd[:, :tc], ALU.mult),
             reads=[rk(t1), rk(rstd)], writes=[rk(t1)])
        p.op("act", lambda h: h.activation(t1[:, :tc], t1[:, :tc], AF.Ln),
             reads=[rk(t1)], writes=[rk(t1)])
        bln = sh.cst(-0.5 * float(np.log(D)))
        p.op("act", lambda h: h.activation(rn[:, :tc], t1[:, :tc], AF.Exp,
                                           scale=-0.5, bias=bln),
             reads=[rk(t1), rk(sh.ctile, sh._cmap[-0.5 * float(np.log(D))])], writes=[rk(rn)])
    o0 = out_off
    for pb in range(PB):
        t = sq[pb]
        p.op("dve", lambda h, t=t, pb=pb: h.tensor_tensor(t[:, :tc], xc[pb][:, :tc],
                                                          mrep[:, :tc], ALU.subtract),
             reads=[rk(xc[pb]), rk(mrep)], writes=[rk(t)])
        if want_xn is not None:
            xn = want_xn[pb]
            p.op("dve", lambda h, t=t, xn=xn: h.tensor_tensor(xn[:, o0:o0 + tc], t[:, :tc],
                                                              rstd[:, :tc], ALU.mult),
                 reads=[rk(t), rk(rstd)], writes=[rk(xn)])
            p.op("act", lambda h, xn=xn, pb=pb: h.copy(xnb[pb][:, o0:o0 + tc], xn[:, o0:o0 + tc]),
                 reads=[rk(xn)], writes=[rk(xnb[pb])])
        else:
            p.op("dve", lambda h, t=t, pb=pb: h.tensor_tensor(xnb[pb][:, o0:o0 + tc], t[:, :tc],
                                                              rstd[:, :tc], ALU.mult),
                 reads=[rk(t), rk(rstd)], writes=[rk(xnb[pb])])


def emit_l1ffn(p, sh, io, ttot=TT, tg=1024, tc=512):
    nc = p.nc
    NG = ttot // tg
    NCH = tg // tc
    HOB = 32

    w1s = p.sbuf("t_w1s", (128, 32)); w2s = p.sbuf("t_w2s", (128, 32))
    h1s = p.sbuf("t_h1s", (128, 32)); h2s = p.sbuf("t_h2s", (128, 32))
    w3sh = p.sbuf("t_w3sh", (128, 8))
    for name, t in [("w1s", w1s), ("w2s", w2s), ("h1s", h1s), ("h2s", h2s)]:
        d = io[name]
        p.op("sp", lambda h, t=t, d=d: h.dma_start(out=t[:], in_=d), writes=[rk(t)], dma=True)
    p.op("sp", lambda h: h.dma_start(out=w3sh[:], in_=io["w3s"]), writes=[rk(w3sh)], dma=True)
    p.op("dve", lambda h: h.tensor_scalar_mul(w3sh[:], w3sh[:], 0.5),
         reads=[rk(w3sh)], writes=[rk(w3sh)])

    xnb = [p.sbuf(f"f_xnb{i}", (128, tg), BF16) for i in range(PB)]
    a1 = [p.sbuf(f"f_a1_{i}", (128, tg), BF16) for i in range(HOB)]
    a2 = [p.sbuf(f"f_a2_{i}", (128, tg), BF16) for i in range(HOB)]
    xc = [p.sbuf(f"f_xc{i}", (128, tc), F32) for i in range(PB)]
    stats = {k: p.sbuf("f_st_" + k, (128, tc), F32) for k in ["mrep", "msq", "varr", "rstd"]}
    wblk = [p.sbuf(f"f_wblk{i}", (128, 1024), BF16) for i in range(2)]
    hblk = [p.sbuf(f"f_hblk{i}", (128, 4096), BF16) for i in range(2)]
    assert tc <= 512
    sg = [sh.scr[0], sh.scr[1]]
    xo = [sh.scr[2], sh.scr[3]]
    pbank = [p.psum(f"f_ps{i}", (128, 512), F32) for i in range(6)]

    def load_wblk(dst, dsrc, nk):
        src = dsrc.rearrange("(kb q) m -> q kb m", q=128)
        d = dst[:, :nk * 128].rearrange("q (kb m) -> q kb m", kb=nk)
        return p.op("pool", lambda h: h.dma_start(out=d, in_=src), writes=[rk(dst)], dma=True)

    for g in range(NG):
        t0 = g * tg
        for c in range(NCH):
            tcs = t0 + c * tc
            for pb in range(PB):
                p.op("sp", lambda h, pb=pb, tcs=tcs: h.dma_start(
                    out=xc[pb][:, :tc], in_=io["xin"][pb * 128:(pb + 1) * 128, tcs:tcs + tc]),
                    writes=[rk(xc[pb])], dma=True)
            emit_ln(p, sh, xc, xnb, pbank[4], pbank[5], stats, tc, out_off=c * tc)
        for ob in range(HOB):
            b = ob % 2
            load_wblk(wblk[b], io["w1t"][ob], 8)
            for c in range(NCH):
                ps = pbank[c % 2]
                def mm1(h, b=b, c=c, ps=ps):
                    for kb in range(PB):
                        i = h.matmul(ps[:, :tc], wblk[b][:, kb * 128:(kb + 1) * 128],
                                     xnb[kb][:, c * tc:(c + 1) * tc],
                                     start=(kb == 0), stop=(kb == PB - 1))
                    return i
                p.op("pe", mm1, reads=[rk(wblk[b])] + [rk(t) for t in xnb], writes=[rk(ps)])
                p.op("act", lambda h, ps=ps, c=c, ob=ob: h.activation(
                    sg[c][:, :tc], ps[:, :tc], AF.Silu, scale=col(w1s, ob)),
                    reads=[rk(ps), rk(w1s)], writes=[rk(sg[c])])
            load_wblk(wblk[b], io["w2t"][ob], 8)
            for c in range(NCH):
                ps = pbank[2 + c % 2]
                def mm2(h, b=b, c=c, ps=ps):
                    for kb in range(PB):
                        i = h.matmul(ps[:, :tc], wblk[b][:, kb * 128:(kb + 1) * 128],
                                     xnb[kb][:, c * tc:(c + 1) * tc],
                                     start=(kb == 0), stop=(kb == PB - 1))
                    return i
                p.op("pe", mm2, reads=[rk(wblk[b])] + [rk(t) for t in xnb], writes=[rk(ps)])
                p.op("dve", lambda h, ps=ps, c=c, ob=ob: h.scalar_tensor_tensor(
                    a1[ob][:, c * tc:(c + 1) * tc], ps[:, :tc], col(w2s, ob),
                    sg[c][:, :tc], ALU.mult, ALU.mult),
                    reads=[rk(ps), rk(w2s), rk(sg[c])], writes=[rk(a1[ob])])
        for ob in range(HOB):
            b = ob % 2
            load_wblk(hblk[b], io["h1t"][ob], 32)
            for c in range(NCH):
                ps = pbank[c % 2]
                def mmh(h, b=b, c=c, ps=ps):
                    for kb in range(HOB):
                        i = h.matmul(ps[:, :tc], hblk[b][:, kb * 128:(kb + 1) * 128],
                                     a1[kb][:, c * tc:(c + 1) * tc],
                                     start=(kb == 0), stop=(kb == HOB - 1))
                    return i
                p.op("pe", mmh, reads=[rk(hblk[b])] + [rk(t) for t in a1], writes=[rk(ps)])
                p.op("act", lambda h, ps=ps, c=c, ob=ob: h.activation(
                    a2[ob][:, c * tc:(c + 1) * tc], ps[:, :tc], AF.Gelu, scale=col(h1s, ob)),
                    reads=[rk(ps), rk(h1s)], writes=[rk(a2[ob])])
        for ob in range(HOB):
            b = ob % 2
            load_wblk(hblk[b], io["h2t"][ob], 32)
            for c in range(NCH):
                ps = pbank[2 + c % 2]
                def mmh2(h, b=b, c=c, ps=ps):
                    for kb in range(HOB):
                        i = h.matmul(ps[:, :tc], hblk[b][:, kb * 128:(kb + 1) * 128],
                                     a2[kb][:, c * tc:(c + 1) * tc],
                                     start=(kb == 0), stop=(kb == HOB - 1))
                    return i
                p.op("pe", mmh2, reads=[rk(hblk[b])] + [rk(t) for t in a2], writes=[rk(ps)])
                p.op("act", lambda h, ps=ps, c=c, ob=ob: h.activation(
                    a1[ob][:, c * tc:(c + 1) * tc], ps[:, :tc], AF.Gelu, scale=col(h2s, ob)),
                    reads=[rk(ps), rk(h2s)], writes=[rk(a1[ob])])
        for db in range(PB):
            b = db % 2
            load_wblk(hblk[b], io["w3t"][db], 32)
            for c in range(NCH):
                tcs = t0 + c * tc
                ps = pbank[c % 2]
                def mmw3(h, b=b, c=c, ps=ps):
                    for kb in range(HOB):
                        i = h.matmul(ps[:, :tc], hblk[b][:, kb * 128:(kb + 1) * 128],
                                     a1[kb][:, c * tc:(c + 1) * tc],
                                     start=(kb == 0), stop=(kb == HOB - 1))
                    return i
                p.op("pe", mmw3, reads=[rk(hblk[b])] + [rk(t) for t in a1], writes=[rk(ps)])
                xg = xc[db % 2]
                p.op("sp", lambda h, db=db, tcs=tcs, xg=xg: h.dma_start(
                    out=xg[:, :tc], in_=io["xin"][db * 128:(db + 1) * 128, tcs:tcs + tc]),
                    writes=[rk(xg)], dma=True)
                xot = xo[c % 2]
                p.op("dve", lambda h, ps=ps, db=db, xg=xg, xot=xot: h.scalar_tensor_tensor(
                    xot[:, :tc], ps[:, :tc], col(w3sh, db), xg[:, :tc], ALU.mult, ALU.add),
                    reads=[rk(ps), rk(w3sh), rk(xg)], writes=[rk(xot)])
                sv = p.op("sp", lambda h, db=db, tcs=tcs, xot=xot: h.dma_start(
                    out=io["xout"][db * 128:(db + 1) * 128, tcs:tcs + tc], in_=xot[:, :tc]),
                    reads=[rk(xot)], dma=True)
                p.store_marker(sv)


def _blk_lhsT(w_t, nob):
    K, O = w_t.shape
    return np.ascontiguousarray(w_t.reshape(K, nob, 128).transpose(1, 0, 2))


def _col_layout(v, nob):
    return np.ascontiguousarray(v.reshape(nob, 128).T)


def prep_l1ffn(I):
    a = {}
    a["w1t"] = _blk_lhsT(I["L1_ffn_w1_w"].T, 32)
    a["w2t"] = _blk_lhsT(I["L1_ffn_w2_w"].T, 32)
    a["h1t"] = _blk_lhsT(I["L1_ffn_h1_w"].T, 32)
    a["h2t"] = _blk_lhsT(I["L1_ffn_h2_w"].T, 32)
    a["w3t"] = _blk_lhsT(I["L1_ffn_w3_w"].T, 8)
    a["w1s"] = _col_layout(I["L1_ffn_w1_s"], 32)
    a["w2s"] = _col_layout(I["L1_ffn_w2_s"], 32)
    a["h1s"] = _col_layout(I["L1_ffn_h1_s"], 32)
    a["h2s"] = _col_layout(I["L1_ffn_h2_s"], 32)
    a["w3s"] = _col_layout(I["L1_ffn_w3_s"], 8)
    return a


ONES128 = np.ones((128, 128), np.float32)


def build_s4(ttot=TT, tg=1024, tc=512):
    p = Prog("s4")
    io = {}
    io["xin"] = p.dram("xin", (D, ttot))
    io["xout"] = p.dram("xout", (D, ttot), kind="ExternalOutput")
    io["w1t"] = p.dram("w1t", (32, 1024, 128))
    io["w2t"] = p.dram("w2t", (32, 1024, 128))
    io["h1t"] = p.dram("h1t", (32, 4096, 128))
    io["h2t"] = p.dram("h2t", (32, 4096, 128))
    io["w3t"] = p.dram("w3t", (8, 4096, 128))
    for s in ["w1s", "w2s", "h1s", "h2s"]:
        io[s] = p.dram(s, (128, 32))
    io["w3s"] = p.dram("w3s", (128, 8))
    ones_d = p.dram("onesc", (128, 128))
    sh = Shared(p)
    sh.load(p, ones_d)
    emit_l1ffn(p, sh, io, ttot=ttot, tg=tg, tc=tc)
    nc = p.build()
    return nc


def emit_l1attn(p, sh, io, ttot=TT, tc=512):
    nc = p.nc
    NCH = ttot // tc
    E20 = float(np.exp(np.float32(20.0)))

    wxp = [p.sbuf(f"a_wxp{i}", (128, 96), BF16) for i in range(PB)]
    wdt = [p.sbuf(f"a_wdt{i}", (64, 128), BF16) for i in range(PB)]
    bdt = p.sbuf("a_bdt", (128, PB), F32)
    Dp = p.sbuf("a_D", (128, PB), F32)
    At = p.sbuf("a_A", (128, 128), F32)
    oh = p.sbuf("a_onehot", (16, 16 * 128), BF16)
    ones5 = p.sbuf("a_ones5", (128, tc), F32)
    p.op("dve", lambda h: h.memset(ones5[:], 1.0), writes=[rk(ones5)])
    for pb in range(PB):
        p.op("pool", lambda h, pb=pb: h.dma_start(out=wxp[pb][:], in_=io["wxpt"][pb]),
             writes=[rk(wxp[pb])], dma=True)
        p.op("pool", lambda h, pb=pb: h.dma_start(out=wdt[pb][:], in_=io["wdtt"][pb]),
             writes=[rk(wdt[pb])], dma=True)
    p.op("sp", lambda h: h.dma_start(out=bdt[:], in_=io["bdt"]), writes=[rk(bdt)], dma=True)
    p.op("sp", lambda h: h.dma_start(out=Dp[:], in_=io["Dp"]), writes=[rk(Dp)], dma=True)
    p.op("sp", lambda h: h.dma_start(out=At[:], in_=io["Alog"]), writes=[rk(At)], dma=True)
    p.op("pool", lambda h: h.dma_start(out=oh[:], in_=io["onehot"]), writes=[rk(oh)], dma=True)
    p.op("act", lambda h: h.activation(At[:], At[:], AF.Exp), reads=[rk(At)], writes=[rk(At)])
    p.op("dve", lambda h: h.tensor_scalar_mul(At[:], At[:], -1.0), reads=[rk(At)], writes=[rk(At)])

    xc = [p.sbuf(f"a_xc{i}", (128, tc), F32) for i in range(PB)]
    xn = [p.sbuf(f"a_xn{i}", (128, tc), F32) for i in range(PB)]
    xnb = [p.sbuf(f"a_xnb{i}", (128, tc), BF16) for i in range(PB)]
    delta = [p.sbuf(f"a_dl{i}", (128, tc), F32) for i in range(PB)]
    dx = [p.sbuf(f"a_dx{i}", (128, tc), F32) for i in range(PB)]
    yac = [p.sbuf(f"a_ya{i}", (128, tc), F32) for i in range(PB)]
    BREP = p.sbuf("a_brep", (128, 16 * tc), BF16)
    CREP = p.sbuf("a_crep", (128, 16 * tc), BF16)
    dtb = p.sbuf("a_dtb", (64, tc), BF16)
    Bb = p.sbuf("a_Bb", (16, tc), BF16)
    Cb = p.sbuf("a_Cb", (16, tc), BF16)
    cLR = [p.sbuf(f"a_clr{i}", (128, 16), F32) for i in range(PB)]
    cS = [p.sbuf(f"a_cs{i}", (128, 16), F32) for i in range(PB)]
    stats = {k: p.sbuf("a_st_" + k, (128, tc), F32) for k in ["mrep", "msq", "varr", "rstd"]}
    ps_dt = p.psum("a_psdt", (64, 512), F32)
    ps_b = p.psum("a_psb", (16, 512), F32)
    ps_c = p.psum("a_psc", (16, 512), F32)
    ps_d = [p.psum(f"a_psd{i}", (128, 512), F32) for i in range(2)]
    ps_r = [p.psum(f"a_psr{i}", (128, 512), F32) for i in range(2)]
    ps_ln = p.psum("a_psln", (128, 512), F32)

    for pb in range(PB):
        p.op("dve", lambda h, pb=pb: h.memset(cLR[pb][:], 0.0), writes=[rk(cLR[pb])])
        p.op("dve", lambda h, pb=pb: h.memset(cS[pb][:], 0.0), writes=[rk(cS[pb])])

    for c in range(NCH):
        t0 = c * tc
        for pb in range(PB):
            p.op("sp", lambda h, pb=pb, t0=t0: h.dma_start(
                out=xc[pb][:, :tc], in_=io["xin"][pb * 128:(pb + 1) * 128, t0:t0 + tc]),
                writes=[rk(xc[pb])], dma=True)
        emit_ln(p, sh, xc, xnb, ps_ln, ps_r[0], stats, tc, want_xn=xn)
        def mm_dt(h):
            for kb in range(PB):
                i = h.matmul(ps_dt[:, :tc], wxp[kb][:, 0:64], xnb[kb][:, :tc],
                             start=(kb == 0), stop=(kb == PB - 1))
            return i
        p.op("pe", mm_dt, reads=[rk(t) for t in wxp] + [rk(t) for t in xnb], writes=[rk(ps_dt)])
        def mm_b(h):
            for kb in range(PB):
                i = h.matmul(ps_b[:, :tc], wxp[kb][:, 64:80], xnb[kb][:, :tc],
                             start=(kb == 0), stop=(kb == PB - 1))
            return i
        p.op("pe", mm_b, reads=[rk(t) for t in wxp] + [rk(t) for t in xnb], writes=[rk(ps_b)])
        def mm_c(h):
            for kb in range(PB):
                i = h.matmul(ps_c[:, :tc], wxp[kb][:, 80:96], xnb[kb][:, :tc],
                             start=(kb == 0), stop=(kb == PB - 1))
            return i
        p.op("pe", mm_c, reads=[rk(t) for t in wxp] + [rk(t) for t in xnb], writes=[rk(ps_c)])
        p.op("act", lambda h: h.copy(dtb[:, :tc], ps_dt[:, :tc]), reads=[rk(ps_dt)], writes=[rk(dtb)])
        p.op("act", lambda h: h.copy(Bb[:, :tc], ps_b[:, :tc]), reads=[rk(ps_b)], writes=[rk(Bb)])
        p.op("act", lambda h: h.copy(Cb[:, :tc], ps_c[:, :tc]), reads=[rk(ps_c)], writes=[rk(Cb)])
        for ob in range(PB):
            ps = ps_d[ob % 2]
            p.op("pe", lambda h, ob=ob, ps=ps: h.matmul(ps[:, :tc], wdt[ob][:, :], dtb[:, :tc],
                                                        start=True, stop=True),
                 reads=[rk(wdt[ob]), rk(dtb)], writes=[rk(ps)])
            sc = sh.scr[ob % 2]
            p.op("act", lambda h, ps=ps, ob=ob, sc=sc: h.activation(
                sc[:, :tc], ps[:, :tc], AF.Exp, bias=col(bdt, ob)),
                reads=[rk(ps), rk(bdt)], writes=[rk(sc)])
            p.op("act", lambda h, ob=ob, sc=sc: h.activation(
                delta[ob][:, :tc], sc[:, :tc], AF.Ln, bias=1.0),
                reads=[rk(sc)], writes=[rk(delta[ob])])
        for n in range(16):
            ps = ps_r[n % 2]
            p.op("pe", lambda h, n=n, ps=ps: h.matmul(
                ps[:, :tc], oh[:, n * 128:(n + 1) * 128], Bb[:, :tc], start=True, stop=True),
                reads=[rk(oh), rk(Bb)], writes=[rk(ps)])
            p.op("act", lambda h, n=n, ps=ps: h.copy(BREP[:, n * tc:(n + 1) * tc], ps[:, :tc]),
                 reads=[rk(ps)], writes=[rk(BREP, n)])
            ps2 = ps_d[n % 2]
            p.op("pe", lambda h, n=n, ps2=ps2: h.matmul(
                ps2[:, :tc], oh[:, n * 128:(n + 1) * 128], Cb[:, :tc], start=True, stop=True),
                reads=[rk(oh), rk(Cb)], writes=[rk(ps2)])
            p.op("act", lambda h, n=n, ps2=ps2: h.copy(CREP[:, n * tc:(n + 1) * tc], ps2[:, :tc]),
                 reads=[rk(ps2)], writes=[rk(CREP, n)])
        for pb in range(PB):
            p.op("dve", lambda h, pb=pb: h.memset(yac[pb][:, :tc], 0.0), writes=[rk(yac[pb])])
            p.op("dve", lambda h, pb=pb: h.tensor_tensor(
                dx[pb][:, :tc], delta[pb][:, :tc], xn[pb][:, :tc], ALU.mult),
                reads=[rk(delta[pb]), rk(xn[pb])], writes=[rk(dx[pb])])
            for n in range(16):
                pt = sh.scr[0]; lr = sh.scr[1]; e1 = sh.scr[2]; R = sh.scr[3]
                G = sh.scr[4]; t2 = sh.scr[5]; S = sh.scr[6]; t3 = sh.scr[7]
                aj = pb * 16 + n
                p.op("dve", lambda h, pb=pb, aj=aj, pt=pt: h.tensor_scalar(
                    pt[:, :tc], delta[pb][:, :tc], col(At, aj), -7.0, ALU.mult, ALU.max),
                    reads=[rk(delta[pb]), rk(At)], writes=[rk(pt)])
                p.op("dve", lambda h, pb=pb, n=n, pt=pt, lr=lr: h.tensor_tensor_scan(
                    lr[:, :tc], ones5[:, :tc], pt[:, :tc], cLR[pb][:, n:n + 1],
                    ALU.mult, ALU.add),
                    reads=[rk(ones5), rk(pt), rk(cLR[pb], n)], writes=[rk(lr)])
                p.op("dve", lambda h, pb=pb, n=n, lr=lr: h.tensor_copy(
                    cLR[pb][:, n:n + 1], lr[:, tc - 1:tc]),
                    reads=[rk(lr)], writes=[rk(cLR[pb], n)])
                p.op("dve", lambda h, lr=lr, e1=e1: h.tensor_scalar(
                    e1[:, :tc], lr[:, :tc], -1.0, 20.0, ALU.mult, ALU.min),
                    reads=[rk(lr)], writes=[rk(e1)])
                p.op("act", lambda h, e1=e1: h.activation(e1[:, :tc], e1[:, :tc], AF.Exp),
                     reads=[rk(e1)], writes=[rk(e1)])
                p.op("act", lambda h, lr=lr, R=R: h.activation(R[:, :tc], lr[:, :tc], AF.Exp),
                     reads=[rk(lr)], writes=[rk(R)])
                p.op("dve", lambda h, pb=pb, n=n, G=G: h.tensor_tensor(
                    G[:, :tc], dx[pb][:, :tc], BREP[:, n * tc:(n + 1) * tc], ALU.mult),
                    reads=[rk(dx[pb]), rk(BREP, n)], writes=[rk(G)])
                p.op("dve", lambda h, e1=e1, G=G, t2=t2: h.tensor_tensor(
                    t2[:, :tc], e1[:, :tc], G[:, :tc], ALU.mult),
                    reads=[rk(e1), rk(G)], writes=[rk(t2)])
                p.op("dve", lambda h, pb=pb, n=n, t2=t2, S=S: h.tensor_tensor_scan(
                    S[:, :tc], ones5[:, :tc], t2[:, :tc], cS[pb][:, n:n + 1],
                    ALU.mult, ALU.add),
                    reads=[rk(ones5), rk(t2), rk(cS[pb], n)], writes=[rk(S)])
                p.op("dve", lambda h, pb=pb, n=n, S=S: h.tensor_copy(
                    cS[pb][:, n:n + 1], S[:, tc - 1:tc]),
                    reads=[rk(S)], writes=[rk(cS[pb], n)])
                p.op("dve", lambda h, n=n, R=R, t3=t3: h.tensor_tensor(
                    t3[:, :tc], CREP[:, n * tc:(n + 1) * tc], R[:, :tc], ALU.mult),
                    reads=[rk(CREP, n), rk(R)], writes=[rk(t3)])
                p.op("dve", lambda h, t3=t3, S=S: h.tensor_tensor(
                    t3[:, :tc], t3[:, :tc], S[:, :tc], ALU.mult),
                    reads=[rk(t3), rk(S)], writes=[rk(t3)])
                p.op("dve", lambda h, pb=pb, t3=t3: h.tensor_tensor(
                    yac[pb][:, :tc], yac[pb][:, :tc], t3[:, :tc], ALU.add),
                    reads=[rk(yac[pb]), rk(t3)], writes=[rk(yac[pb])])
        for pb in range(PB):
            q = sh.scr[0]
            p.op("dve", lambda h, pb=pb, q=q: h.scalar_tensor_tensor(
                q[:, :tc], xn[pb][:, :tc], col(Dp, pb), yac[pb][:, :tc], ALU.mult, ALU.add),
                reads=[rk(xn[pb]), rk(Dp), rk(yac[pb])], writes=[rk(q)])
            xo = sh.scr[1]
            p.op("dve", lambda h, pb=pb, q=q, xo=xo: h.tensor_tensor(
                xo[:, :tc], xc[pb][:, :tc], q[:, :tc], ALU.add),
                reads=[rk(xc[pb]), rk(q)], writes=[rk(xo)])
            sv = p.op("sp", lambda h, pb=pb, t0=t0, xo=xo: h.dma_start(
                out=io["xout"][pb * 128:(pb + 1) * 128, t0:t0 + tc], in_=xo[:, :tc]),
                reads=[rk(xo)], dma=True)
            p.store_marker(sv)


def prep_l1attn(I):
    a = {}
    wxp = I["L1_attn_W_xproj"]
    a["wxpt"] = np.ascontiguousarray(wxp.reshape(8, 128, 96))
    wdt = I["L1_attn_W_dt"]
    a["wdtt"] = np.ascontiguousarray(wdt.reshape(64, 8, 128).transpose(1, 0, 2))
    a["bdt"] = _col_layout(I["L1_attn_b_dt"], 8)
    a["Dp"] = _col_layout(I["L1_attn_D"], 8)
    al = I["L1_attn_A_log"]
    a["Alog"] = np.ascontiguousarray(al.reshape(8, 128, 16).transpose(1, 0, 2).reshape(128, 128))
    oh = np.zeros((16, 16, 128), np.float32)
    for n in range(16):
        oh[n, n, :] = 1.0
    a["onehot"] = oh.reshape(16, 16 * 128).astype(np.float32)
    return a


def build_s5(ttot=TT, tc=512):
    p = Prog("s5")
    io = {}
    io["xin"] = p.dram("xin", (D, ttot))
    io["xout"] = p.dram("xout", (D, ttot), kind="ExternalOutput")
    io["wxpt"] = p.dram("wxpt", (8, 128, 96))
    io["wdtt"] = p.dram("wdtt", (8, 64, 128))
    io["bdt"] = p.dram("bdt", (128, 8))
    io["Dp"] = p.dram("Dp", (128, 8))
    io["Alog"] = p.dram("Alog", (128, 128))
    io["onehot"] = p.dram("onehot", (16, 16 * 128))
    ones_d = p.dram("onesc", (128, 128))
    sh = Shared(p)
    sh.load(p, ones_d)
    emit_l1attn(p, sh, io, ttot=ttot, tc=tc)
    nc = p.build()
    return nc


def emit_moe(p, sh, io, pref, ttot=TT, tg=512, tc=256):
    nc = p.nc
    NG = ttot // tg
    NCH = tg // tc
    E = 16

    idb = p.sbuf(pref + "_idb", (128, 128), BF16)
    oh = p.sbuf(pref + "_oh", (16, 16 * 128), BF16)
    gwt = [p.sbuf(pref + f"_gwt{i}", (128, 16), F32) for i in range(PB)]
    gwp = [p.sbuf(pref + f"_gwp{i}", (128, 128), F32) for i in range(PB)]
    ones16 = p.sbuf(pref + "_o16", (16, 128), F32)
    rn_rep = p.sbuf(pref + "_rnrep", (128, 16), F32)
    rn16 = p.sbuf(pref + "_rn16", (16, 1), F32)
    gwraw = p.sbuf(pref + "_gwraw", (16, 1024), F32)
    p.op("pool", lambda h: h.dma_start(out=idb[:], in_=io["ident"]), writes=[rk(idb)], dma=True)
    p.op("pool", lambda h: h.dma_start(out=oh[:], in_=io["onehot"]), writes=[rk(oh)], dma=True)
    p.op("sp", lambda h: h.dma_start(out=gwraw[:], in_=io["gw_raw"]), writes=[rk(gwraw)], dma=True)
    p.op("sp", lambda h: h.dma_start(out=ones16[:], in_=io["ones16"]), writes=[rk(ones16)], dma=True)
    for kb in range(PB):
        p.op("sp", lambda h, kb=kb: h.dma_start(out=gwt[kb][:], in_=io["gwt"][kb]),
             writes=[rk(gwt[kb])], dma=True)
        p.op("sp", lambda h, kb=kb: h.dma_start(out=gwp[kb][:], in_=io["gwp"][kb]),
             writes=[rk(gwp[kb])], dma=True)
    ps0 = p.psum(pref + "_ps0", (128, 512), F32)
    ps1 = p.psum(pref + "_ps1", (128, 512), F32)
    gsq = p.sbuf(pref + "_gsq", (16, 1024), F32)
    t16 = sh.scr[0]
    p.op("act", lambda h: h.activation(gsq[:, :], gwraw[:, :], AF.Square),
         reads=[rk(gwraw)], writes=[rk(gsq)])
    p.op("dve", lambda h: h.tensor_reduce(rn16[:, 0:1], gsq[:, :], AX.X, ALU.add),
         reads=[rk(gsq)], writes=[rk(rn16)])
    p.op("act", lambda h: h.activation(rn16[:, 0:1], rn16[:, 0:1], AF.Ln),
         reads=[rk(rn16)], writes=[rk(rn16)])
    p.op("act", lambda h: h.activation(rn16[:, 0:1], rn16[:, 0:1], AF.Exp, scale=-0.5),
         reads=[rk(rn16)], writes=[rk(rn16)])
    def mm_rn(h):
        for kb in range(PB):
            i = h.matmul(ps0[:, 0:16], sh.ones[:], t16[:, kb * 16:(kb + 1) * 16],
                         start=(kb == 0), stop=(kb == PB - 1))
        return i
    for kb in range(PB):
        p.op("act", lambda h, kb=kb: h.activation(t16[:, kb * 16:(kb + 1) * 16],
                                                  gwt[kb][:, :], AF.Square),
             reads=[rk(gwt[kb])], writes=[rk(t16)])
    p.op("pe", mm_rn, reads=[rk(sh.ones), rk(t16)],
         writes=[rk(ps0)])
    p.op("act", lambda h: h.activation(rn_rep[:, :], ps0[:, 0:16], AF.Ln),
         reads=[rk(ps0)], writes=[rk(rn_rep)])
    p.op("act", lambda h: h.activation(rn_rep[:, :], rn_rep[:, :], AF.Exp, scale=-0.5),
         reads=[rk(rn_rep)], writes=[rk(rn_rep)])

    xnb = [p.sbuf(pref + f"_xnb{i}", (128, tg), BF16) for i in range(PB)]
    hh = [[p.sbuf(pref + f"_hh{e}_{hb}", (128, tg), BF16) for hb in range(4)] for e in range(E)]
    shh = [p.sbuf(pref + f"_shh{hb}", (128, tg), BF16) for hb in range(4)]
    combT = p.sbuf(pref + "_combT", (16, tg), BF16)
    maskT = p.sbuf(pref + "_maskT", (16, tg), BF16)
    rnormg = p.sbuf(pref + "_rnormg", (16, tg), F32)
    wgu = [p.sbuf(pref + f"_wgu{i}", (128, 4096), BF16) for i in range(2)]
    wdt = p.sbuf(pref + "_wdt", (128, 68 * 128), BF16)
    xc = [p.sbuf(pref + f"_xc{i}", (128, tc), F32) for i in range(PB)]
    stats = {k: p.sbuf(pref + "_st_" + k, (128, tc), F32)
             for k in ["mrep", "msq", "varr", "rstd", "rnorm"]}
    lgt = p.sbuf(pref + "_lgt", (128, 16), F32)
    cmp_t = p.sbuf(pref + "_cmp", (128, 256), F32)
    rank = p.sbuf(pref + "_rank", (128, 16), F32)
    maskb = p.sbuf(pref + "_maskb", (128, 16), BF16)
    ps_y = [p.psum(pref + f"_psy{i}", (128, 512), F32) for i in range(2)]
    ps_g = p.psum(pref + "_psg", (128, 512), F32)
    ps_u = p.psum(pref + "_psu", (128, 512), F32)
    crep = [p.sbuf(pref + f"_crepsb{i}", (128, 512), BF16) for i in range(NCH)]
    ps_t = p.psum(pref + "_pst", (16, 1024), BF16)

    for g in range(NG):
        t0g = g * tg
        for c in range(NCH):
            t0 = t0g + c * tc
            for pb in range(PB):
                p.op("sp", lambda h, pb=pb, t0=t0: h.dma_start(
                    out=xc[pb][:, :tc], in_=io["xin"][pb * 128:(pb + 1) * 128, t0:t0 + tc]),
                    writes=[rk(xc[pb])], dma=True)
            xn = sh.scr[:8]
            emit_ln(p, sh, xc, xnb, ps0, ps1, stats, tc, out_off=c * tc,
                    want_xn=xn, want_rnorm=True)
            p.op("act", lambda h, c=c: h.copy(rnormg[:, c * tc:(c + 1) * tc],
                                              stats["rnorm"][:16, :tc]),
                 reads=[rk(stats["rnorm"])], writes=[rk(rnormg, c)])
            for tb in range(tc // 128):
                def mm_lgt(h, c=c, tb=tb):
                    for kb in range(PB):
                        i = h.matmul(ps0[:, 0:16], xn[kb][:, tb * 128:(tb + 1) * 128],
                                     gwt[kb][:, :], start=(kb == 0), stop=(kb == PB - 1))
                    return i
                p.op("pe", mm_lgt,
                     reads=[rk(t) for t in xn] + [rk(t) for t in gwt], writes=[rk(ps0)])
                p.op("dve", lambda h: h.tensor_tensor(lgt[:, :], ps0[:, 0:16], rn_rep[:, :],
                                                      ALU.mult),
                     reads=[rk(ps0), rk(rn_rep)], writes=[rk(lgt)])
                A3 = lgt[:, :].unsqueeze(2).broadcast_to([128, 16, 16])
                B3 = lgt[:, :].unsqueeze(1).broadcast_to([128, 16, 16])
                p.op("dve", lambda h, A3=A3, B3=B3: h.tensor_tensor(
                    cmp_t[:, :].rearrange("q (a b) -> q a b", a=16), A3, B3, ALU.is_lt),
                    reads=[rk(lgt)], writes=[rk(cmp_t)])
                p.op("dve", lambda h: h.tensor_reduce(
                    rank[:, :], cmp_t[:, :].rearrange("q (a b) -> q a b", a=16), AX.X, ALU.add),
                    reads=[rk(cmp_t)], writes=[rk(rank)])
                p.op("dve", lambda h: h.tensor_scalar(maskb[:, :], rank[:, :], 8.0, None,
                                                      ALU.is_lt),
                     reads=[rk(rank)], writes=[rk(maskb)])
                p.op("pe", lambda h: h.transpose(ps_t[:, 0:128], maskb[:, :], idb[:]),
                     reads=[rk(maskb), rk(idb)], writes=[rk(ps_t)])
                p.op("act", lambda h, c=c, tb=tb: h.copy(
                    maskT[:, c * tc + tb * 128: c * tc + (tb + 1) * 128], ps_t[:16, 0:128]),
                    reads=[rk(ps_t)], writes=[rk(maskT, (c, tb))])
            def mm_lge(h, c=c):
                for kb in range(PB):
                    i = h.matmul(ps1[:, :tc], gwp[kb][:, :], xn[kb][:, :tc],
                                 start=(kb == 0), stop=(kb == PB - 1))
                return i
            p.op("pe", mm_lge, reads=[rk(t) for t in xn] + [rk(t) for t in gwp],
                 writes=[rk(ps1)])
            w1 = sh.scr[8]
            p.op("dve", lambda h: h.tensor_scalar(w1[:16, :tc], ps1[:16, :tc],
                                                  rn16[:, 0:1], None, ALU.mult),
                 reads=[rk(ps1), rk(rn16)], writes=[rk(w1)])
            p.op("dve", lambda h, c=c: h.tensor_tensor(
                w1[:16, :tc], w1[:16, :tc], rnormg[:, c * tc:(c + 1) * tc], ALU.mult),
                reads=[rk(w1), rk(rnormg, c)], writes=[rk(w1)])
            p.op("act", lambda h: h.activation(w1[:16, :tc], w1[:16, :tc], AF.Sigmoid),
                 reads=[rk(w1)], writes=[rk(w1)])
            p.op("dve", lambda h, c=c: h.tensor_tensor(
                w1[:16, :tc], w1[:16, :tc],
                maskT[:, c * tc:(c + 1) * tc], ALU.mult),
                reads=[rk(w1)] + [rk(maskT, (c, tb)) for tb in range(tc // 128)],
                writes=[rk(w1)])
            p.op("pe", lambda h: h.matmul(ps0[:, :tc], ones16[:, :], w1[:16, :tc],
                                          start=True, stop=True),
                 reads=[rk(ones16), rk(w1)], writes=[rk(ps0)])
            rt = sh.scr[0]
            p.op("dve", lambda h: h.reciprocal(rt[:16, :tc], ps0[:16, :tc]),
                 reads=[rk(ps0)], writes=[rk(rt)])
            p.op("dve", lambda h, c=c: h.tensor_tensor(
                combT[:, c * tc:(c + 1) * tc], w1[:16, :tc], rt[:16, :tc], ALU.mult),
                reads=[rk(w1), rk(rt)], writes=[rk(combT, c)])
        for e in range(E):
            p.op("pool", lambda h, e=e: h.dma_start(
                out=wgu[0][:, :].rearrange("q (hb kb m) -> q hb kb m", hb=4, kb=PB),
                in_=io["wg"][e].rearrange("hb (kb q) m -> q hb kb m", q=128)),
                writes=[rk(wgu[0])], dma=True)
            p.op("pool", lambda h, e=e: h.dma_start(
                out=wgu[1][:, :].rearrange("q (hb kb m) -> q hb kb m", hb=4, kb=PB),
                in_=io["wu"][e].rearrange("hb (kb q) m -> q hb kb m", q=128)),
                writes=[rk(wgu[1])], dma=True)
            for c in range(NCH):
                p.op("pe", lambda h, e=e, c=c: h.matmul(
                    ps0[:, :tc], oh[:, e * 128:(e + 1) * 128],
                    combT[:, c * tc:(c + 1) * tc], start=True, stop=True),
                    reads=[rk(oh), rk(combT, c)], writes=[rk(ps0)])
                p.op("act", lambda h, c=c: h.copy(crep[c][:, :tc], ps0[:, :tc]),
                     reads=[rk(ps0)], writes=[rk(crep[c])])
            for hb in range(4):
                for c in range(NCH):
                    def mm_g(h, hb=hb, c=c):
                        for kb in range(PB):
                            i = h.matmul(ps_g[:, :tc],
                                         wgu[0][:, (hb * PB + kb) * 128:(hb * PB + kb + 1) * 128],
                                         xnb[kb][:, c * tc:(c + 1) * tc],
                                         start=(kb == 0), stop=(kb == PB - 1))
                        return i
                    p.op("pe", mm_g, reads=[rk(wgu[0])] + [rk(t) for t in xnb],
                         writes=[rk(ps_g)])
                    def mm_u(h, hb=hb, c=c):
                        for kb in range(PB):
                            i = h.matmul(ps_u[:, :tc],
                                         wgu[1][:, (hb * PB + kb) * 128:(hb * PB + kb + 1) * 128],
                                         xnb[kb][:, c * tc:(c + 1) * tc],
                                         start=(kb == 0), stop=(kb == PB - 1))
                        return i
                    p.op("pe", mm_u, reads=[rk(wgu[1])] + [rk(t) for t in xnb],
                         writes=[rk(ps_u)])
                    s1 = sh.scr[c]
                    p.op("act", lambda h, s1=s1: h.activation(s1[:, :tc], ps_g[:, :tc], AF.Silu),
                         reads=[rk(ps_g)], writes=[rk(s1)])
                    s2 = sh.scr[2 + c]
                    p.op("dve", lambda h, s1=s1, s2=s2: h.tensor_tensor(
                        s2[:, :tc], s1[:, :tc], ps_u[:, :tc], ALU.mult),
                        reads=[rk(s1), rk(ps_u)], writes=[rk(s2)])
                    p.op("dve", lambda h, e=e, hb=hb, c=c, s2=s2: h.tensor_tensor(
                        hh[e][hb][:, c * tc:(c + 1) * tc], s2[:, :tc], crep[c][:, :tc],
                        ALU.mult),
                        reads=[rk(s2), rk(crep[c])], writes=[rk(hh[e][hb], c)])
        p.op("pool", lambda h: h.dma_start(
            out=wgu[0][:, :].rearrange("q (hb kb m) -> q hb kb m", hb=4, kb=PB),
            in_=io["swg"].rearrange("hb (kb q) m -> q hb kb m", q=128)),
            writes=[rk(wgu[0])], dma=True)
        p.op("pool", lambda h: h.dma_start(
            out=wgu[1][:, :].rearrange("q (hb kb m) -> q hb kb m", hb=4, kb=PB),
            in_=io["swu"].rearrange("hb (kb q) m -> q hb kb m", q=128)),
            writes=[rk(wgu[1])], dma=True)
        for hb in range(4):
            for c in range(NCH):
                def mm_g(h, hb=hb, c=c):
                    for kb in range(PB):
                        i = h.matmul(ps_g[:, :tc],
                                     wgu[0][:, (hb * PB + kb) * 128:(hb * PB + kb + 1) * 128],
                                     xnb[kb][:, c * tc:(c + 1) * tc],
                                     start=(kb == 0), stop=(kb == PB - 1))
                    return i
                p.op("pe", mm_g, reads=[rk(wgu[0])] + [rk(t) for t in xnb], writes=[rk(ps_g)])
                def mm_u(h, hb=hb, c=c):
                    for kb in range(PB):
                        i = h.matmul(ps_u[:, :tc],
                                     wgu[1][:, (hb * PB + kb) * 128:(hb * PB + kb + 1) * 128],
                                     xnb[kb][:, c * tc:(c + 1) * tc],
                                     start=(kb == 0), stop=(kb == PB - 1))
                    return i
                p.op("pe", mm_u, reads=[rk(wgu[1])] + [rk(t) for t in xnb], writes=[rk(ps_u)])
                s1 = sh.scr[c]
                p.op("act", lambda h, s1=s1: h.activation(s1[:, :tc], ps_g[:, :tc], AF.Silu),
                     reads=[rk(ps_g)], writes=[rk(s1)])
                p.op("dve", lambda h, hb=hb, c=c, s1=s1: h.tensor_tensor(
                    shh[hb][:, c * tc:(c + 1) * tc], s1[:, :tc], ps_u[:, :tc], ALU.mult),
                    reads=[rk(s1), rk(ps_u)], writes=[rk(shh[hb], c)])
        for db in range(PB):
            p.op("pool", lambda h, db=db: h.dma_start(
                out=wdt[:, :].rearrange("q (j m) -> q j m", j=68),
                in_=io["wd"][db].rearrange("(j q) m -> q j m", q=128)),
                writes=[rk(wdt)], dma=True)
            for c in range(NCH):
                t0 = t0g + c * tc
                psy = ps_y[c]
                def mm_d(h, db=db, c=c, psy=psy):
                    j = 0
                    for e in range(E):
                        for hb in range(4):
                            i = h.matmul(psy[:, :tc], wdt[:, j * 128:(j + 1) * 128],
                                         hh[e][hb][:, c * tc:(c + 1) * tc],
                                         start=(j == 0), stop=False)
                            j += 1
                    for hb in range(4):
                        i = h.matmul(psy[:, :tc], wdt[:, j * 128:(j + 1) * 128],
                                     shh[hb][:, c * tc:(c + 1) * tc],
                                     start=False, stop=(hb == 3))
                        j += 1
                    return i
                p.op("pe", mm_d,
                     reads=[rk(wdt)] + [rk(hh[e][hb], c) for e in range(E) for hb in range(4)]
                     + [rk(shh[hb], c) for hb in range(4)],
                     writes=[rk(psy)])
                xg = xc[db % 2]
                p.op("sp", lambda h, db=db, t0=t0, xg=xg: h.dma_start(
                    out=xg[:, :tc], in_=io["xin"][db * 128:(db + 1) * 128, t0:t0 + tc]),
                    writes=[rk(xg)], dma=True)
                xo = sh.scr[4 + (c % 2)]
                p.op("dve", lambda h, psy=psy, xg=xg, xo=xo: h.scalar_tensor_tensor(
                    xo[:, :tc], psy[:, :tc], 0.5, xg[:, :tc], ALU.mult, ALU.add),
                    reads=[rk(psy), rk(xg)], writes=[rk(xo)])
                sv = p.op("sp", lambda h, db=db, t0=t0, xo=xo: h.dma_start(
                    out=io["xout"][db * 128:(db + 1) * 128, t0:t0 + tc], in_=xo[:, :tc]),
                    reads=[rk(xo)], dma=True)
                p.store_marker(sv)


def prep_moe(I):
    a = {}
    wg = I["L0_ffn_w_gate"]
    a["wg"] = np.ascontiguousarray(
        wg.reshape(16, 1024, 4, 128).transpose(0, 2, 1, 3))
    a["wu"] = np.ascontiguousarray(
        I["L0_ffn_w_up"].reshape(16, 1024, 4, 128).transpose(0, 2, 1, 3))
    wd = I["L0_ffn_w_down"]
    wdb = wd.reshape(16, 4, 128, 8, 128)
    swd = I["L0_ffn_sw_down"].reshape(4, 128, 8, 128)
    wdall = np.zeros((8, 68, 128, 128), np.float32)
    for db in range(8):
        j = 0
        for e in range(16):
            for hb in range(4):
                wdall[db, j] = wdb[e, hb, :, db, :]
                j += 1
        for hb in range(4):
            wdall[db, j] = swd[hb, :, db, :]
            j += 1
    a["wd"] = np.ascontiguousarray(wdall.reshape(8, 68 * 128, 128))
    a["swg"] = np.ascontiguousarray(
        I["L0_ffn_sw_gate"].reshape(1024, 4, 128).transpose(1, 0, 2))
    a["swu"] = np.ascontiguousarray(
        I["L0_ffn_sw_up"].reshape(1024, 4, 128).transpose(1, 0, 2))
    gw = I["L0_ffn_gate_w"]
    a["gw_raw"] = np.ascontiguousarray(gw)
    a["gwt"] = np.ascontiguousarray(gw.T.reshape(8, 128, 16))
    gwp = np.zeros((1024, 128), np.float32)
    gwp[:, :16] = gw.T
    a["gwp"] = np.ascontiguousarray(gwp.reshape(8, 128, 128))
    oh = np.zeros((16, 16, 128), np.float32)
    for n in range(16):
        oh[n, n, :] = 1.0
    a["onehot_m"] = oh.reshape(16, 16 * 128)
    a["ident"] = np.eye(128, dtype=np.float32)
    a["ones16"] = np.ones((16, 128), np.float32)
    return a


def build_s1(ttot=TT, tg=512, tc=256):
    p = Prog("s1")
    io = {}
    io["xin"] = p.dram("xin", (D, ttot))
    io["xout"] = p.dram("xout", (D, ttot), kind="ExternalOutput")
    io["wg"] = p.dram("wg", (16, 4, 1024, 128))
    io["wu"] = p.dram("wu", (16, 4, 1024, 128))
    io["wd"] = p.dram("wd", (8, 68 * 128, 128))
    io["swg"] = p.dram("swg", (4, 1024, 128))
    io["swu"] = p.dram("swu", (4, 1024, 128))
    io["gw_raw"] = p.dram("gw_raw", (16, 1024))
    io["gwt"] = p.dram("gwt", (8, 128, 16))
    io["gwp"] = p.dram("gwp", (8, 128, 128))
    io["onehot"] = p.dram("onehot", (16, 16 * 128))
    io["ident"] = p.dram("ident", (128, 128))
    io["ones16"] = p.dram("ones16", (16, 128))
    ones_d = p.dram("onesc", (128, 128))
    sh = Shared(p)
    sh.load(p, ones_d)
    emit_moe(p, sh, io, "m1", ttot=ttot, tg=tg, tc=tc)
    nc = p.build()
    return nc


def emit_mamba(p, sh, io, ttot=TT, tc=256):
    nc = p.nc
    NCH = ttot // tc
    DI = 2048
    IPB = DI // 128

    xpt = [p.sbuf(f"b_xpt{i}", (128, 96), BF16) for i in range(IPB)]
    dtpt = [p.sbuf(f"b_dtpt{i}", (64, 128), BF16) for i in range(IPB)]
    convw = p.sbuf("b_convw", (128, IPB * 4), F32)
    convb = p.sbuf("b_convb", (128, IPB), F32)
    dtbb = p.sbuf("b_dtbb", (128, IPB), F32)
    Dpar = p.sbuf("b_Dpar", (128, IPB), F32)
    At = p.sbuf("b_At", (128, IPB * 16), F32)
    oh = p.sbuf("b_oh", (16, 16 * 128), BF16)
    for i in range(IPB):
        p.op("pool", lambda h, i=i: h.dma_start(out=xpt[i][:], in_=io["xpt"][i]),
             writes=[rk(xpt[i])], dma=True)
        p.op("pool", lambda h, i=i: h.dma_start(out=dtpt[i][:], in_=io["dtpt"][i]),
             writes=[rk(dtpt[i])], dma=True)
    p.op("sp", lambda h: h.dma_start(out=convw[:], in_=io["convw"]), writes=[rk(convw)], dma=True)
    p.op("sp", lambda h: h.dma_start(out=convb[:], in_=io["convb"]), writes=[rk(convb)], dma=True)
    p.op("sp", lambda h: h.dma_start(out=dtbb[:], in_=io["dtb"]), writes=[rk(dtbb)], dma=True)
    p.op("sp", lambda h: h.dma_start(out=Dpar[:], in_=io["Dpar"]), writes=[rk(Dpar)], dma=True)
    p.op("sp", lambda h: h.dma_start(out=At[:], in_=io["Alog"]), writes=[rk(At)], dma=True)
    p.op("pool", lambda h: h.dma_start(out=oh[:], in_=io["onehot"]), writes=[rk(oh)], dma=True)
    p.op("act", lambda h: h.activation(At[:], At[:], AF.Exp), reads=[rk(At)], writes=[rk(At)])
    p.op("dve", lambda h: h.tensor_scalar_mul(At[:], At[:], -1.0), reads=[rk(At)], writes=[rk(At)])

    xc = [p.sbuf(f"b_xc{i}", (128, tc), F32) for i in range(PB)]
    xnb = [p.sbuf(f"b_xnb{i}", (128, tc), BF16) for i in range(PB)]
    xi = [p.sbuf(f"b_xi{i}", (128, tc + 3), BF16) for i in range(IPB)]
    sz = [p.sbuf(f"b_sz{i}", (128, tc), BF16) for i in range(IPB)]
    u = [p.sbuf(f"b_u{i}", (128, tc), F32) for i in range(IPB)]
    ub = [p.sbuf(f"b_ub{i}", (128, tc), BF16) for i in range(IPB)]
    delta = [p.sbuf(f"b_dl{i}", (128, tc), F32) for i in range(IPB)]
    du = [p.sbuf(f"b_du{i}", (128, tc), BF16) for i in range(IPB)]
    yac = [p.sbuf(f"b_ya{i}", (128, tc), F32) for i in range(IPB)]
    ybt = [p.sbuf(f"b_yb{i}", (128, tc), BF16) for i in range(IPB)]
    BREP = p.sbuf("b_brep", (128, 16 * tc), BF16)
    CREP = p.sbuf("b_crep", (128, 16 * tc), BF16)
    bscr = p.sbuf("b_bscr", (128, tc), BF16)
    dtb = p.sbuf("b_dtb", (64, tc), BF16)
    Bb = p.sbuf("b_Bb", (16, tc), BF16)
    Cb = p.sbuf("b_Cb", (16, tc), BF16)
    hcar = [p.sbuf(f"b_hc{i}", (128, 16), F32) for i in range(IPB)]
    stats = {k: p.sbuf("b_st_" + k, (128, tc), F32) for k in ["mrep", "msq", "varr", "rstd"]}
    ib = [p.sbuf(f"b_ib{i}", (128, 1024), BF16) for i in range(2)]
    ob_t = [p.sbuf(f"b_ob{i}", (128, 2048), BF16) for i in range(2)]
    pp = [p.psum(f"b_pp{i}", (128, 512), F32) for i in range(7)]

    for i in range(IPB):
        p.op("dve", lambda h, i=i: h.memset(hcar[i][:], 0.0), writes=[rk(hcar[i])])
        p.op("dve", lambda h, i=i: h.memset(xi[i][:, 0:3], 0.0), writes=[rk(xi[i])])

    for c in range(NCH):
        t0 = c * tc
        for pb in range(PB):
            p.op("sp", lambda h, pb=pb, t0=t0: h.dma_start(
                out=xc[pb][:, :tc], in_=io["xin"][pb * 128:(pb + 1) * 128, t0:t0 + tc]),
                writes=[rk(xc[pb])], dma=True)
        emit_ln(p, sh, xc, xnb, pp[0], pp[1], stats, tc)
        for obi in range(32):
            b = ib[obi % 2]
            p.op("pool", lambda h, obi=obi, b=b: h.dma_start(
                out=b[:, :].rearrange("q (kb m) -> q kb m", kb=PB),
                in_=io["inpt"][obi].rearrange("(kb q) m -> q kb m", q=128)),
                writes=[rk(b)], dma=True)
            ps = pp[2 + obi % 2]
            def mm_ip(h, b=b, ps=ps):
                for kb in range(PB):
                    i = h.matmul(ps[:, :tc], b[:, kb * 128:(kb + 1) * 128],
                                 xnb[kb][:, :tc], start=(kb == 0), stop=(kb == PB - 1))
                return i
            p.op("pe", mm_ip, reads=[rk(b)] + [rk(t) for t in xnb], writes=[rk(ps)])
            if obi < IPB:
                p.op("act", lambda h, obi=obi, ps=ps: h.copy(xi[obi][:, 3:3 + tc], ps[:, :tc]),
                     reads=[rk(ps)], writes=[rk(xi[obi])])
            else:
                p.op("act", lambda h, obi=obi, ps=ps: h.activation(
                    sz[obi - IPB][:, :tc], ps[:, :tc], AF.Silu),
                    reads=[rk(ps)], writes=[rk(sz[obi - IPB])])
        for i in range(IPB):
            s0 = sh.scr[i % 2]
            p.op("dve", lambda h, i=i, s0=s0: h.tensor_scalar(
                s0[:, :tc], xi[i][:, 0:tc], col(convw, i * 4 + 0), None, ALU.mult),
                reads=[rk(xi[i])], writes=[rk(s0)])
            for k in range(1, 4):
                p.op("dve", lambda h, i=i, k=k, s0=s0: h.scalar_tensor_tensor(
                    s0[:, :tc], xi[i][:, k:k + tc], col(convw, i * 4 + k), s0[:, :tc],
                    ALU.mult, ALU.add),
                    reads=[rk(xi[i]), rk(s0)], writes=[rk(s0)])
            p.op("act", lambda h, i=i, s0=s0: h.activation(
                u[i][:, :tc], s0[:, :tc], AF.Silu, bias=col(convb, i)),
                reads=[rk(s0), rk(convb)], writes=[rk(u[i])])
            p.op("act", lambda h, i=i: h.copy(ub[i][:, :tc], u[i][:, :tc]),
                 reads=[rk(u[i])], writes=[rk(ub[i])])
            p.op("dve", lambda h, i=i: h.tensor_copy(xi[i][:, 0:3], xi[i][:, tc:tc + 3]),
                 reads=[rk(xi[i])], writes=[rk(xi[i])])
        def mm_xd(h, lo, hi, ps):
            for kb in range(IPB):
                i = h.matmul(ps, xpt[kb][:, lo:hi], ub[kb][:, :tc],
                             start=(kb == 0), stop=(kb == IPB - 1))
            return i
        p.op("pe", lambda h: mm_xd(h, 0, 64, pp[4][0:64, :tc]),
             reads=[rk(t) for t in xpt] + [rk(t) for t in ub], writes=[rk(pp[4])])
        p.op("pe", lambda h: mm_xd(h, 64, 80, pp[5][0:16, :tc]),
             reads=[rk(t) for t in xpt] + [rk(t) for t in ub], writes=[rk(pp[5])])
        p.op("pe", lambda h: mm_xd(h, 80, 96, pp[6][0:16, :tc]),
             reads=[rk(t) for t in xpt] + [rk(t) for t in ub], writes=[rk(pp[6])])
        p.op("act", lambda h: h.copy(dtb[:, :tc], pp[4][0:64, :tc]), reads=[rk(pp[4])], writes=[rk(dtb)])
        p.op("act", lambda h: h.copy(Bb[:, :tc], pp[5][0:16, :tc]), reads=[rk(pp[5])], writes=[rk(Bb)])
        p.op("act", lambda h: h.copy(Cb[:, :tc], pp[6][0:16, :tc]), reads=[rk(pp[6])], writes=[rk(Cb)])
        for i in range(IPB):
            ps = pp[2 + i % 2]
            p.op("pe", lambda h, i=i, ps=ps: h.matmul(ps[:, :tc], dtpt[i][:, :], dtb[:, :tc],
                                                      start=True, stop=True),
                 reads=[rk(dtpt[i]), rk(dtb)], writes=[rk(ps)])
            s0 = sh.scr[2 + i % 2]
            p.op("act", lambda h, i=i, ps=ps, s0=s0: h.activation(
                s0[:, :tc], ps[:, :tc], AF.Exp, bias=col(dtbb, i)),
                reads=[rk(ps), rk(dtbb)], writes=[rk(s0)])
            p.op("act", lambda h, i=i, s0=s0: h.activation(
                delta[i][:, :tc], s0[:, :tc], AF.Ln, bias=1.0),
                reads=[rk(s0)], writes=[rk(delta[i])])
        for n in range(16):
            ps = pp[5 - n % 2]
            p.op("pe", lambda h, n=n, ps=ps: h.matmul(
                ps[:, :tc], oh[:, n * 128:(n + 1) * 128], Bb[:, :tc], start=True, stop=True),
                reads=[rk(oh), rk(Bb)], writes=[rk(ps)])
            p.op("act", lambda h, n=n, ps=ps: h.copy(BREP[:, n * tc:(n + 1) * tc], ps[:, :tc]),
                 reads=[rk(ps)], writes=[rk(BREP, n)])
            ps2 = pp[2 + n % 2]
            p.op("pe", lambda h, n=n, ps2=ps2: h.matmul(
                ps2[:, :tc], oh[:, n * 128:(n + 1) * 128], Cb[:, :tc], start=True, stop=True),
                reads=[rk(oh), rk(Cb)], writes=[rk(ps2)])
            p.op("act", lambda h, n=n, ps2=ps2: h.copy(CREP[:, n * tc:(n + 1) * tc], ps2[:, :tc]),
                 reads=[rk(ps2)], writes=[rk(CREP, n)])
        for i in range(IPB):
            p.op("dve", lambda h, i=i: h.tensor_tensor(du[i][:, :tc], delta[i][:, :tc],
                                                       u[i][:, :tc], ALU.mult),
                 reads=[rk(delta[i]), rk(u[i])], writes=[rk(du[i])])
            for n in range(16):
                sa = sh.scr[4 + n % 2]; hs = sh.scr[6 + n % 2]; t3 = sh.scr[8]
                p.op("act", lambda h, i=i, n=n, sa=sa: h.activation(
                    sa[:, :tc], delta[i][:, :tc], AF.Exp, scale=col(At, i * 16 + n)),
                    reads=[rk(delta[i]), rk(At)], writes=[rk(sa)])
                p.op("dve", lambda h, i=i, n=n: h.tensor_tensor(
                    bscr[:, :tc], du[i][:, :tc], BREP[:, n * tc:(n + 1) * tc], ALU.mult),
                    reads=[rk(du[i]), rk(BREP, n)], writes=[rk(bscr)])
                p.op("dve", lambda h, i=i, n=n, sa=sa, hs=hs: h.tensor_tensor_scan(
                    hs[:, :tc], sa[:, :tc], bscr[:, :tc], hcar[i][:, n:n + 1],
                    ALU.mult, ALU.add),
                    reads=[rk(sa), rk(bscr), rk(hcar[i], n)], writes=[rk(hs)])
                p.op("dve", lambda h, i=i, n=n, hs=hs: h.tensor_copy(
                    hcar[i][:, n:n + 1], hs[:, tc - 1:tc]),
                    reads=[rk(hs)], writes=[rk(hcar[i], n)])
                if n == 0:
                    p.op("dve", lambda h, i=i, n=n, hs=hs: h.tensor_tensor(
                        yac[i][:, :tc], CREP[:, n * tc:(n + 1) * tc], hs[:, :tc], ALU.mult),
                        reads=[rk(CREP, n), rk(hs)], writes=[rk(yac[i])])
                else:
                    p.op("dve", lambda h, i=i, n=n, hs=hs, t3=t3: h.tensor_tensor(
                        t3[:, :tc], CREP[:, n * tc:(n + 1) * tc], hs[:, :tc], ALU.mult),
                        reads=[rk(CREP, n), rk(hs)], writes=[rk(t3)])
                    p.op("dve", lambda h, i=i, t3=t3: h.tensor_tensor(
                        yac[i][:, :tc], yac[i][:, :tc], t3[:, :tc], ALU.add),
                        reads=[rk(yac[i]), rk(t3)], writes=[rk(yac[i])])
        for i in range(IPB):
            q = sh.scr[3]
            p.op("dve", lambda h, i=i, q=q: h.scalar_tensor_tensor(
                q[:, :tc], u[i][:, :tc], col(Dpar, i), yac[i][:, :tc], ALU.mult, ALU.add),
                reads=[rk(u[i]), rk(Dpar), rk(yac[i])], writes=[rk(q)])
            p.op("dve", lambda h, i=i, q=q: h.tensor_tensor(
                ybt[i][:, :tc], q[:, :tc], sz[i][:, :tc], ALU.mult),
                reads=[rk(q), rk(sz[i])], writes=[rk(ybt[i])])
        for obi in range(PB):
            b = ob_t[obi % 2]
            p.op("pool", lambda h, obi=obi, b=b: h.dma_start(
                out=b[:, :].rearrange("q (kb m) -> q kb m", kb=IPB),
                in_=io["outpt"][obi].rearrange("(kb q) m -> q kb m", q=128)),
                writes=[rk(b)], dma=True)
            ps = pp[2 + obi % 2]
            def mm_op(h, b=b, ps=ps):
                for kb in range(IPB):
                    i = h.matmul(ps[:, :tc], b[:, kb * 128:(kb + 1) * 128],
                                 ybt[kb][:, :tc], start=(kb == 0), stop=(kb == IPB - 1))
                return i
            p.op("pe", mm_op, reads=[rk(b)] + [rk(t) for t in ybt], writes=[rk(ps)])
            xo = sh.scr[obi % 2]
            p.op("dve", lambda h, obi=obi, ps=ps, xo=xo: h.tensor_tensor(
                xo[:, :tc], xc[obi][:, :tc], ps[:, :tc], ALU.add),
                reads=[rk(xc[obi]), rk(ps)], writes=[rk(xo)])
            sv = p.op("sp", lambda h, obi=obi, t0=t0, xo=xo: h.dma_start(
                out=io["xout"][obi * 128:(obi + 1) * 128, t0:t0 + tc], in_=xo[:, :tc]),
                reads=[rk(xo)], dma=True)
            p.store_marker(sv)


def prep_mamba(I):
    a = {}
    a["inpt"] = _blk_lhsT(I["L0_attn_h_in_proj_w"].T, 32)
    a["outpt"] = _blk_lhsT(I["L0_attn_h_out_proj_w"].T, 8)
    xp = I["L0_attn_h_x_proj_w"].T
    a["xpt"] = np.ascontiguousarray(xp.reshape(16, 128, 96))
    dtp = I["L0_attn_h_dt_proj_w"].T
    a["dtpt"] = np.ascontiguousarray(dtp.reshape(64, 16, 128).transpose(1, 0, 2))
    cw = I["L0_attn_h_conv_w"]
    a["convw"] = np.ascontiguousarray(cw.reshape(16, 128, 4).transpose(1, 0, 2).reshape(128, 64))
    a["convb"] = _col_layout(I["L0_attn_h_conv_b"], 16)
    a["dtb"] = _col_layout(I["L0_attn_h_dt_proj_b"], 16)
    a["Dpar"] = _col_layout(I["L0_attn_h_D_param"], 16)
    al = I["L0_attn_h_A_log"]
    a["Alog"] = np.ascontiguousarray(
        al.reshape(16, 128, 16).transpose(1, 0, 2).reshape(128, 256))
    oh = np.zeros((16, 16, 128), np.float32)
    for n in range(16):
        oh[n, n, :] = 1.0
    a["onehot_b"] = oh.reshape(16, 16 * 128)
    return a


def build_s2(ttot=TT, tc=256):
    p = Prog("s2")
    io = {}
    io["xin"] = p.dram("xin", (D, ttot))
    io["xout"] = p.dram("xout", (D, ttot), kind="ExternalOutput")
    io["inpt"] = p.dram("inpt", (32, 1024, 128))
    io["outpt"] = p.dram("outpt", (8, 2048, 128))
    io["xpt"] = p.dram("xpt", (16, 128, 96))
    io["dtpt"] = p.dram("dtpt", (16, 64, 128))
    io["convw"] = p.dram("convw", (128, 64))
    io["convb"] = p.dram("convb", (128, 16))
    io["dtb"] = p.dram("dtb", (128, 16))
    io["Dpar"] = p.dram("Dpar", (128, 16))
    io["Alog"] = p.dram("Alog", (128, 256))
    io["onehot"] = p.dram("onehot", (16, 16 * 128))
    ones_d = p.dram("onesc", (128, 128))
    sh = Shared(p)
    sh.load(p, ones_d)
    emit_mamba(p, sh, io, ttot=ttot, tc=tc)
    nc = p.build()
    return nc


_PROG_CACHE = {}


def _get_prog(name, builder):
    if name not in _PROG_CACHE:
        _PROG_CACHE[name] = builder()
    return _PROG_CACHE[name]


def kernel(**inputs):
    I = inputs
    x = np.asarray(I["L0_attn_hidden"], dtype=np.float32)
    xs = [np.ascontiguousarray(x[b].T) for b in range(N_CORES)]

    moe_a = prep_moe(I)
    mam_a = prep_mamba(I)
    ffn_a = prep_l1ffn(I)
    att_a = prep_l1attn(I)

    nc_s1 = _get_prog("s1", build_s1)
    nc_s2 = _get_prog("s2", build_s2)
    nc_s4 = _get_prog("s4", build_s4)
    nc_s5 = _get_prog("s5", build_s5)

    def launch(nc, extra, xs_in):
        in_maps = []
        for b in range(N_CORES):
            m = dict(extra)
            m["xin"] = xs_in[b]
            m["onesc"] = ONES128
            in_maps.append(m)
        res = run_bass_kernel_spmd(nc, in_maps, core_ids=list(range(N_CORES)))
        return [np.asarray(res.results[b]["xout"]) for b in range(N_CORES)]

    moe_map = {k: moe_a[k] for k in ["wg", "wu", "wd", "swg", "swu", "gw_raw",
                                     "gwt", "gwp", "ident", "ones16"]}
    moe_map["onehot"] = moe_a["onehot_m"]
    mam_map = {k: mam_a[k] for k in ["inpt", "outpt", "xpt", "dtpt", "convw",
                                     "convb", "dtb", "Dpar", "Alog"]}
    mam_map["onehot"] = mam_a["onehot_b"]
    att_map = {k: att_a[k] for k in ["wxpt", "wdtt", "bdt", "Dp", "Alog", "onehot"]}

    xs = launch(nc_s1, moe_map, xs)
    xs = launch(nc_s2, mam_map, xs)
    xs = launch(nc_s1, moe_map, xs)
    xs = launch(nc_s4, ffn_a, xs)
    xs = launch(nc_s5, att_map, xs)
    xs = launch(nc_s4, ffn_a, xs)

    out = np.stack([xs[b].T for b in range(N_CORES)], axis=0)
    return np.ascontiguousarray(out.astype(np.float32))
